# version 02 (final)
# baseline (speedup 1.0000x reference)
import numpy as np
import ml_dtypes
import concourse.bass as bass
import concourse.mybir as mybir
from concourse.bass_utils import run_bass_kernel_spmd

BFNP = ml_dtypes.bfloat16
F32 = mybir.dt.float32
BF16 = mybir.dt.bfloat16
AF = mybir.ActivationFunctionType
ALU = mybir.AluOpType
AX = mybir.AxisListType

NC = 8
D = 1024
T = 4096
B = 16
BPC = B // NC
DC = D // 128
TS = 512

SEMS = ("ld", "ld1", "ld2", "mm", "vec", "act", "st", "st1", "gp")


class Ctx:

    def __init__(self, E, eng, sems):
        self.E = E
        self.eng = eng
        self.sems = sems
        self.c = {k: 0 for k in sems}

    def on(self, owner):
        return self.E == owner

    def dma(self, q, sem, out, in_):
        if self.E == q:
            self.eng.dma_start(out=out, in_=in_).then_inc(self.sems[sem], 16)
        self.c[sem] += 16
        return self.c[sem]

    def op(self, owner, sem, fn):
        auto = "vec" if owner == "vector" else ("act" if owner == "scalar" else None)
        if auto is not None:
            if self.E == owner:
                self.eng.wait_ge(self.sems[auto], self.c[auto])
                ins = fn(self.eng)
                ins.then_inc(self.sems[auto], 1)
            self.c[auto] += 1
            return self.c[auto]
        ins = None
        if self.E == owner:
            ins = fn(self.eng)
        if sem is not None:
            self.c[sem] += 1
            if ins is not None:
                ins.then_inc(self.sems[sem], 1)
        return self.c[sem] if sem is not None else None

    def w(self, owner, sem, val=None):
        if val is None:
            val = self.c[sem]
        if self.E == owner:
            self.eng.wait_ge(self.sems[sem], val)


def make_block(nc, body):
    sems = {k: nc.alloc_semaphore(k) for k in SEMS}
    with nc.Block() as block:
        @block.sync
        def _(e):
            body(Ctx("sync", e, sems))

        @block.tensor
        def _(e):
            body(Ctx("tensor", e, sems))

        @block.vector
        def _(e):
            body(Ctx("vector", e, sems))

        @block.scalar
        def _(e):
            body(Ctx("scalar", e, sems))

        @block.gpsimd
        def _(e):
            body(Ctx("gpsimd", e, sems))
    return nc


def new_nc():
    return bass.Bass("TRN2", target_bir_lowering=False)


def build_pw():
    nc = new_nc()
    specs = [
        ("fcw", 8 * 1024, 4096),
        ("pjw", 8 * 4096, 1024),
        ("w1", 2 * 1024, 5460),
        ("w2", 2 * 2730, 1024),
        ("wqkv", 2 * 1024, 1536),
        ("wo", 2 * 512, 1024),
        ("wq", 1024, 1024), ("wk", 1024, 1024), ("wv", 1024, 1024),
        ("wg", 1024, 1024), ("woo", 1024, 1024),
    ]
    ins, outs = {}, {}
    for n, r, c in specs:
        ins[n] = nc.dram_tensor(n, [r, c], F32, kind="ExternalInput").ap()
        outs[n] = nc.dram_tensor(n + "_bf", [r, c], BF16, kind="ExternalOutput").ap()
    gw = nc.dram_tensor("gw", [1024, 8], F32, kind="ExternalInput").ap()
    gwhi = nc.dram_tensor("gw_hi", [1024, 8], BF16, kind="ExternalOutput").ap()
    gwlo = nc.dram_tensor("gw_lo", [1024, 8], BF16, kind="ExternalOutput").ap()

    W = 5504
    f32b = [nc.alloc_sbuf_tensor(f"f32b{i}", [128, W], F32) for i in range(2)]
    bfb = [nc.alloc_sbuf_tensor(f"bfb{i}", [128, W], BF16) for i in range(2)]
    glo = nc.alloc_sbuf_tensor("glo", [128, 8], BF16)

    def body(x: Ctx):
        k = 0
        jobs = []
        for n, r, c in specs:
            for r0 in range(0, r, 128):
                rr = min(128, r - r0)
                jobs.append((n, r0, rr, c))
        for (n, r0, rr, c) in jobs:
            p = k % 2
            lds, sts = ("ld", "st") if p == 0 else ("ld1", "st1")
            x.w("sync", sts, x.c[sts])
            x.dma("sync", lds, f32b[p][:rr, :c], ins[n][r0:r0 + rr, :])
            x.w("vector", lds, x.c[lds])
            x.w("vector", sts, x.c[sts])
            x.op("vector", "vec", lambda e, p=p, rr=rr, c=c: e.tensor_copy(out=bfb[p][:rr, :c], in_=f32b[p][:rr, :c]))
            x.w("gpsimd", "vec", x.c["vec"])
            x.dma("gpsimd", sts, outs[n][r0:r0 + rr, :], bfb[p][:rr, :c])
            k += 1
        for j in range(8):
            p = k % 2
            lds, sts = ("ld", "st") if p == 0 else ("ld1", "st1")
            x.w("sync", "st", x.c["st"])
            x.w("sync", "st1", x.c["st1"])
            x.dma("sync", lds, f32b[p][:, :8], gw[j * 128:(j + 1) * 128, :])
            x.w("vector", lds, x.c[lds])
            x.w("vector", "st", x.c["st"])
            x.w("vector", "st1", x.c["st1"])
            x.op("vector", None, lambda e, p=p: e.tensor_copy(out=bfb[p][:, :8], in_=f32b[p][:, :8]))
            x.op("vector", "vec", lambda e, p=p: e.tensor_sub(out=glo[:, :8], in0=f32b[p][:, :8], in1=bfb[p][:, :8]))
            x.w("gpsimd", "vec", x.c["vec"])
            x.dma("gpsimd", sts, gwhi[j * 128:(j + 1) * 128, :], bfb[p][:, :8])
            x.dma("gpsimd", sts, gwlo[j * 128:(j + 1) * 128, :], glo[:, :8])
            k += 1
        x.w("gpsimd", "st")
        x.w("gpsimd", "st1")
    return make_block(nc, body), None


def build_pa():
    nc = new_nc()
    xT = nc.dram_tensor("xT", [BPC, D, T], F32, kind="ExternalInput").ap()
    oh = nc.dram_tensor("xn_hi", [BPC, D, T], BF16, kind="ExternalOutput").ap()
    ol = nc.dram_tensor("xn_lo", [BPC, D, T], BF16, kind="ExternalOutput").ap()
    ones = nc.dram_tensor("cb_ones", [128, 1], F32, kind="ExternalInput").ap()
    onesT = nc.dram_tensor("cb_onesT", [1, 128], F32, kind="ExternalInput").ap()

    xs = [nc.alloc_sbuf_tensor(f"xs{i}", [128, TS], F32) for i in range(DC)]
    sq = nc.alloc_sbuf_tensor("sq", [128, TS], F32)
    on = nc.alloc_sbuf_tensor("on", [128, 1], F32)
    onT = nc.alloc_sbuf_tensor("onT", [1, 128], F32)
    stat = nc.alloc_sbuf_tensor("stat", [1, 3 * TS], F32)
    red = nc.alloc_sbuf_tensor("red", [1, 2 * TS], F32)
    hi = nc.alloc_sbuf_tensor("hi", [128, TS], BF16)
    lo = nc.alloc_sbuf_tensor("lo", [128, TS], BF16)
    xn = nc.alloc_sbuf_tensor("xn", [128, TS], F32)
    ps_st = nc.alloc_psum_tensor("ps_st", [1, 2 * TS], F32)
    ps_bc = nc.alloc_psum_tensor("ps_bc", [128, 2 * TS], F32)

    eps = 1e-5

    def body(x: Ctx):
        x.dma("sync", "ld1", on[:, :], ones[:, :])
        x.dma("sync", "ld1", onT[:, :], onesT[:, :])
        it = 0
        for b in range(BPC):
            for t0 in range(0, T, TS):
                x.w("sync", "st", 16 * (2 * DC) * it)
                for i in range(DC):
                    x.dma("sync", "ld", xs[i][:, :], xT[b, i * 128:(i + 1) * 128, t0:t0 + TS])
                ldtar = x.c["ld"]
                x.w("tensor", "ld", ldtar)
                x.w("tensor", "ld1", x.c["ld1"])
                x.w("tensor", "vec")
                for i in range(DC):
                    x.op("tensor", "mm" if i == DC - 1 else None, lambda e, i=i: e.matmul(
                        ps_st[:, 0:TS], on[:, :], xs[i][:, :], start=(i == 0), stop=(i == DC - 1)))
                x.w("scalar", "ld", ldtar)
                for i in range(DC):
                    if i > 0:
                        x.w("scalar", "mm", x.c["mm"])
                    x.op("scalar", "act", lambda e, i=i: e.activation(out=sq[:, :], in_=xs[i][:, :], func=AF.Square))
                    x.w("tensor", "act", x.c["act"])
                    x.op("tensor", "mm", lambda e, i=i: e.matmul(
                        ps_st[:, TS:2 * TS], on[:, :], sq[:, :], start=(i == 0), stop=(i == DC - 1)))
                mmtar = x.c["mm"]
                x.w("vector", "mm", mmtar)
                x.op("vector", None, lambda e: e.tensor_copy(out=stat[:, 0:2 * TS], in_=ps_st[:, :]))
                x.op("vector", None, lambda e: e.tensor_scalar_mul(out=stat[:, 0:TS], in0=stat[:, 0:TS], scalar1=1.0 / D))
                x.op("vector", None, lambda e: e.tensor_mul(out=stat[:, 2 * TS:3 * TS], in0=stat[:, 0:TS], in1=stat[:, 0:TS]))
                x.op("vector", None, lambda e: e.tensor_scalar_mul(out=stat[:, TS:2 * TS], in0=stat[:, TS:2 * TS], scalar1=1.0 / D))
                x.op("vector", None, lambda e: e.tensor_sub(out=stat[:, TS:2 * TS], in0=stat[:, TS:2 * TS], in1=stat[:, 2 * TS:3 * TS]))
                x.op("vector", "vec", lambda e: e.tensor_scalar_add(out=stat[:, TS:2 * TS], in0=stat[:, TS:2 * TS], scalar1=eps))
                x.w("scalar", "vec", x.c["vec"])
                x.op("scalar", "act", lambda e: e.activation(out=stat[:, 2 * TS:3 * TS], in_=stat[:, TS:2 * TS], func=AF.Sqrt))
                x.w("vector", "act", x.c["act"])
                x.op("vector", None, lambda e: e.reciprocal(out=red[:, TS:2 * TS], in_=stat[:, 2 * TS:3 * TS]))
                x.op("vector", None, lambda e: e.tensor_mul(out=red[:, 0:TS], in0=stat[:, 0:TS], in1=red[:, TS:2 * TS]))
                x.op("vector", "vec", lambda e: e.tensor_scalar_mul(out=red[:, 0:TS], in0=red[:, 0:TS], scalar1=-1.0))
                x.w("tensor", "vec", x.c["vec"])
                x.op("tensor", None, lambda e: e.matmul(ps_bc[:, 0:TS], onT[:, :], red[:, 0:TS], start=True, stop=True))
                x.op("tensor", "mm", lambda e: e.matmul(ps_bc[:, TS:2 * TS], onT[:, :], red[:, TS:2 * TS], start=True, stop=True))
                x.w("vector", "mm", x.c["mm"])
                for i in range(DC):
                    x.op("vector", None, lambda e, i=i: e.tensor_mul(out=xn[:, :], in0=xs[i][:, :], in1=ps_bc[:, TS:2 * TS]))
                    x.op("vector", None, lambda e, i=i: e.tensor_add(out=xn[:, :], in0=xn[:, :], in1=ps_bc[:, 0:TS]))
                    x.op("vector", None, lambda e, i=i: e.tensor_copy(out=hi[:, :], in_=xn[:, :]))
                    x.op("vector", "vec", lambda e, i=i: e.tensor_sub(out=lo[:, :], in0=xn[:, :], in1=hi[:, :]))
                    x.w("gpsimd", "vec", x.c["vec"])
                    x.dma("gpsimd", "st", oh[b, i * 128:(i + 1) * 128, t0:t0 + TS], hi[:, :])
                    x.dma("gpsimd", "st", ol[b, i * 128:(i + 1) * 128, t0:t0 + TS], lo[:, :])
                    x.w("vector", "st", x.c["st"])
                it += 1
        x.w("gpsimd", "st")
        x.w("sync", "st")
    return make_block(nc, body)


CHB = 64

def build_pb():
    nc = new_nc()
    xch = nc.dram_tensor("xch", [CHB, 128, 32, 16], BF16, kind="ExternalInput").ap()
    xcl = nc.dram_tensor("xcl", [CHB, 128, 32, 16], BF16, kind="ExternalInput").ap()
    tcm = nc.dram_tensor("tcm", [CHB, 4352], F32, kind="ExternalInput").ap()
    yc = nc.dram_tensor("yc", [CHB, T, 16], F32, kind="ExternalOutput").ap()
    thi = nc.dram_tensor("thi", [CHB, 4352], BF16, kind="Internal")
    tlo = nc.dram_tensor("tlo", [CHB, 4352], BF16, kind="Internal")
    thi_ap, tlo_ap = thi.ap(), tlo.ap()

    tf = nc.alloc_sbuf_tensor("tf", [CHB, 4352], F32)
    th = nc.alloc_sbuf_tensor("th", [CHB, 4352], BF16)
    tl = nc.alloc_sbuf_tensor("tl", [CHB, 4352], BF16)
    Thi = [nc.alloc_sbuf_tensor(f"Thi{i}", [128, 4096], BF16) for i in range(2)]
    Tlo = [nc.alloc_sbuf_tensor(f"Tlo{i}", [128, 4096], BF16) for i in range(2)]
    xh = [nc.alloc_sbuf_tensor(f"xh{i}", [128, 512], BF16) for i in range(2)]
    xl = [nc.alloc_sbuf_tensor(f"xl{i}", [128, 512], BF16) for i in range(2)]
    ys = [nc.alloc_sbuf_tensor(f"ys{i}", [128, 512], F32) for i in range(2)]
    ps = [nc.alloc_psum_tensor(f"ps{i}", [128, 512], F32) for i in range(2)]

    def body(x: Ctx):
        x.dma("sync", "ld", tf[:, :], tcm[:, :])
        x.w("vector", "ld", 16)
        x.op("vector", None, lambda e: e.tensor_copy(out=th[:, :], in_=tf[:, :]))
        x.op("vector", "vec", lambda e: e.tensor_sub(out=tl[:, :], in0=tf[:, :], in1=th[:, :]))
        x.w("gpsimd", "vec", x.c["vec"])
        x.dma("gpsimd", "st", thi_ap[:, :], th[:, :])
        x.dma("gpsimd", "st", tlo_ap[:, :], tl[:, :])
        vbase = x.c["vec"]
        for d in range(CHB):
            p = d % 2
            if d < 2:
                x.w("sync", "st", 32)
            x.w("sync", "mm", max(0, d - 1))
            lds, sts = ("ld", "st") if p == 0 else ("ld1", "st1")
            src_h = bass.AP(thi, d * 4352 + 1, [[1, 128], [128, 32], [1, 128]])
            src_l = bass.AP(tlo, d * 4352 + 1, [[1, 128], [128, 32], [1, 128]])
            x.dma("sync", lds, Thi[p][:, :], src_h)
            x.dma("sync", lds, Tlo[p][:, :], src_l)
            x.dma("sync", lds, xh[p][:, :], xch[d, :, :, :])
            x.dma("sync", lds, xl[p][:, :], xcl[d, :, :, :])
            ldtar = x.c[lds]
            x.w("tensor", lds, ldtar)
            x.w("tensor", "vec", vbase + max(0, d - 1))
            for dl in range(32):
                nmov = (32 - dl) * 16
                x.op("tensor", None, lambda e, dl=dl, p=p, nmov=nmov: e.matmul(
                    ps[p][:, 16 * dl:16 * dl + nmov], Thi[p][:, 128 * dl:128 * dl + 128],
                    xh[p][:, 0:nmov], start=(dl == 0), stop=False))
                x.op("tensor", None, lambda e, dl=dl, p=p, nmov=nmov: e.matmul(
                    ps[p][:, 16 * dl:16 * dl + nmov], Thi[p][:, 128 * dl:128 * dl + 128],
                    xl[p][:, 0:nmov], start=False, stop=False))
                x.op("tensor", "mm" if dl == 31 else None, lambda e, dl=dl, p=p, nmov=nmov: e.matmul(
                    ps[p][:, 16 * dl:16 * dl + nmov], Tlo[p][:, 128 * dl:128 * dl + 128],
                    xh[p][:, 0:nmov], start=False, stop=(dl == 31)))
            x.w("vector", "mm", x.c["mm"])
            x.w("vector", sts, x.c[sts])
            x.op("vector", "vec", lambda e, p=p: e.tensor_copy(out=ys[p][:, :], in_=ps[p][:, :]))
            x.w("gpsimd", "vec", x.c["vec"])
            dst = bass.AP(yc.tensor, d * (T * 16), [[16, 128], [2048, 32], [1, 16]])
            x.dma("gpsimd", sts, dst, ys[p][:, :])
        x.w("gpsimd", "st")
        x.w("gpsimd", "st1")
        x.w("sync", "st")
        x.w("sync", "st1")
    return make_block(nc, body)


def ln_alloc(nc, L, tag=""):
    k = {}
    k["L"] = L
    k["sq"] = nc.alloc_sbuf_tensor(f"ln_sq{tag}", [128, L], F32)
    k["stat"] = nc.alloc_sbuf_tensor(f"ln_stat{tag}", [1, 3 * L], F32)
    k["red"] = nc.alloc_sbuf_tensor(f"ln_red{tag}", [1, 2 * L], F32)
    k["ps_st"] = nc.alloc_psum_tensor(f"ln_ps_st{tag}", [1, 2 * L], F32)
    k["ps_bc"] = nc.alloc_psum_tensor(f"ln_ps_bc{tag}", [128, 2 * L], F32)
    return k


def emit_ln(x: Ctx, k, srcs, on, onT, eps, consume, affine=None):
    L = k["L"]
    x.w("tensor", "vec")
    x.w("scalar", "vec")
    for i in range(DC):
        x.op("tensor", "mm" if i == DC - 1 else None, lambda e, i=i: e.matmul(
            k["ps_st"][:, 0:L], on[:, :], srcs[i], start=(i == 0), stop=(i == DC - 1)))
    for i in range(DC):
        if i > 0:
            x.w("scalar", "mm", x.c["mm"])
        x.op("scalar", "act", lambda e, i=i: e.activation(out=k["sq"][:, :], in_=srcs[i], func=AF.Square))
        x.w("tensor", "act", x.c["act"])
        x.op("tensor", "mm", lambda e, i=i: e.matmul(
            k["ps_st"][:, L:2 * L], on[:, :], k["sq"][:, :], start=(i == 0), stop=(i == DC - 1)))
    st, red = k["stat"], k["red"]
    x.w("vector", "mm", x.c["mm"])
    x.op("vector", None, lambda e: e.tensor_copy(out=st[:, 0:2 * L], in_=k["ps_st"][:, :]))
    x.op("vector", None, lambda e: e.tensor_scalar_mul(out=st[:, 0:L], in0=st[:, 0:L], scalar1=1.0 / D))
    x.op("vector", None, lambda e: e.tensor_mul(out=st[:, 2 * L:3 * L], in0=st[:, 0:L], in1=st[:, 0:L]))
    x.op("vector", None, lambda e: e.tensor_scalar_mul(out=st[:, L:2 * L], in0=st[:, L:2 * L], scalar1=1.0 / D))
    x.op("vector", None, lambda e: e.tensor_sub(out=st[:, L:2 * L], in0=st[:, L:2 * L], in1=st[:, 2 * L:3 * L]))
    x.op("vector", "vec", lambda e: e.tensor_scalar_add(out=st[:, L:2 * L], in0=st[:, L:2 * L], scalar1=eps))
    x.w("scalar", "vec", x.c["vec"])
    x.op("scalar", "act", lambda e: e.activation(out=st[:, 2 * L:3 * L], in_=st[:, L:2 * L], func=AF.Sqrt))
    x.w("vector", "act", x.c["act"])
    x.op("vector", None, lambda e: e.reciprocal(out=red[:, L:2 * L], in_=st[:, 2 * L:3 * L]))
    x.op("vector", None, lambda e: e.tensor_mul(out=red[:, 0:L], in0=st[:, 0:L], in1=red[:, L:2 * L]))
    x.op("vector", "vec", lambda e: e.tensor_scalar_mul(out=red[:, 0:L], in0=red[:, 0:L], scalar1=-1.0))
    x.w("tensor", "vec", x.c["vec"])
    x.op("tensor", None, lambda e: e.matmul(k["ps_bc"][:, 0:L], onT[:, :], red[:, 0:L], start=True, stop=True))
    x.op("tensor", "mm", lambda e: e.matmul(k["ps_bc"][:, L:2 * L], onT[:, :], red[:, L:2 * L], start=True, stop=True))
    x.w("vector", "mm", x.c["mm"])
    for i in range(DC):
        consume(x, i, k["ps_bc"][:, L:2 * L], k["ps_bc"][:, 0:L])


NTOK = 2048

def build_pc():
    nc = new_nc()
    xnh_d = nc.dram_tensor("xnh", [D, NTOK], BF16, kind="ExternalInput").ap()
    xnl_d = nc.dram_tensor("xnl", [D, NTOK], BF16, kind="ExternalInput").ap()
    xT_d = nc.dram_tensor("xTs", [D, NTOK], F32, kind="ExternalInput").ap()
    yc_d = nc.dram_tensor("ycs", [D, NTOK], F32, kind="ExternalInput").ap()
    fcw = nc.dram_tensor("fcw", [8, D, 4096], BF16, kind="ExternalInput").ap()
    pjw = nc.dram_tensor("pjw", [8, 4096, D], BF16, kind="ExternalInput").ap()
    gwh = nc.dram_tensor("gwh", [D, 8], BF16, kind="ExternalInput").ap()
    gwl = nc.dram_tensor("gwl", [D, 8], BF16, kind="ExternalInput").ap()
    ones = nc.dram_tensor("cb_ones", [128, 1], F32, kind="ExternalInput").ap()
    onesT = nc.dram_tensor("cb_onesT", [1, 128], F32, kind="ExternalInput").ap()
    onehot = nc.dram_tensor("cb_onehot", [8, 8 * 128], F32, kind="ExternalInput").ap()
    ident = nc.dram_tensor("cb_ident", [128, 128], F32, kind="ExternalInput").ap()
    x1_d = nc.dram_tensor("x1s", [D, NTOK], F32, kind="ExternalOutput").ap()
    oh_d = nc.dram_tensor("xn1h", [D, NTOK], BF16, kind="ExternalOutput").ap()
    ol_d = nc.dram_tensor("xn1l", [D, NTOK], BF16, kind="ExternalOutput").ap()

    xnh = [nc.alloc_sbuf_tensor(f"xnh{i}", [128, TS], BF16) for i in range(DC)]
    xnl = [nc.alloc_sbuf_tensor(f"xnl{i}", [128, TS], BF16) for i in range(DC)]
    acc = [nc.alloc_sbuf_tensor(f"acc{i}", [128, TS], F32) for i in range(DC)]
    hsb = [nc.alloc_sbuf_tensor(f"hsb{i}", [128, TS], BF16) for i in range(8)]
    wfc = [nc.alloc_sbuf_tensor(f"wfc{i}", [128, 1024], BF16) for i in range(DC)]
    wpj = [nc.alloc_sbuf_tensor(f"wpj{i}", [128, 1024], BF16) for i in range(8)]
    gws = nc.alloc_sbuf_tensor("gws", [128, 16 * 8], BF16)
    oh_sb = nc.alloc_sbuf_tensor("ohot", [8, 8 * 128], F32)
    idn = nc.alloc_sbuf_tensor("idn", [128, 128], F32)
    on = nc.alloc_sbuf_tensor("on", [128, 1], F32)
    onT = nc.alloc_sbuf_tensor("onT", [1, 128], F32)
    l2t = nc.alloc_sbuf_tensor("l2t", [128, 8], F32)
    keep = nc.alloc_sbuf_tensor("keep", [128, 8], F32)
    pex = nc.alloc_sbuf_tensor("pex", [128, 8], F32)
    wts = nc.alloc_sbuf_tensor("wts", [128, 8], F32)
    mrow = nc.alloc_sbuf_tensor("mrow", [128, 4], F32)
    wtsT = nc.alloc_sbuf_tensor("wtsT", [8, TS], F32)
    wb = nc.alloc_sbuf_tensor("wb", [128, TS], F32)
    tmp = nc.alloc_sbuf_tensor("tmpv", [128, TS], F32)
    xs = nc.alloc_sbuf_tensor("xs", [128, TS], F32)
    ycs = nc.alloc_sbuf_tensor("ycs_sb", [128, TS], F32)
    hi = nc.alloc_sbuf_tensor("hi", [128, TS], BF16)
    lo = nc.alloc_sbuf_tensor("lo", [128, TS], BF16)
    xnf = nc.alloc_sbuf_tensor("xnf", [128, TS], F32)
    lnk = ln_alloc(nc, TS)
    ps_g = nc.alloc_psum_tensor("ps_g", [128, 8], F32)
    ps_h = nc.alloc_psum_tensor("ps_h", [128, TS], F32)
    ps_o = nc.alloc_psum_tensor("ps_o", [128, TS], F32)
    ps_t = ps_h
    ps_w = ps_o

    def body(x: Ctx):
        x.dma("sync", "ld1", on[:, :], ones[:, :])
        x.dma("sync", "ld1", onT[:, :], onesT[:, :])
        x.dma("sync", "ld1", oh_sb[:, :], onehot[:, :])
        x.dma("sync", "ld1", idn[:, :], ident[:, :])
        for i in range(DC):
            x.dma("sync", "ld1", gws[:, i * 16:i * 16 + 8], gwh[i * 128:(i + 1) * 128, :])
            x.dma("sync", "ld1", gws[:, i * 16 + 8:i * 16 + 16], gwl[i * 128:(i + 1) * 128, :])
        for tb in range(NTOK // TS):
            t0 = tb * TS
            x.w("sync", "st", x.c["st"])
            x.w("sync", "vec", x.c["vec"])
            for i in range(DC):
                x.dma("sync", "ld", xnh[i][:, :], xnh_d[i * 128:(i + 1) * 128, t0:t0 + TS])
                x.dma("sync", "ld", xnl[i][:, :], xnl_d[i * 128:(i + 1) * 128, t0:t0 + TS])
            ldtar = x.c["ld"]
            for sc in range(TS // 128):
                x.w("tensor", "ld", ldtar)
                x.w("tensor", "ld1", x.c["ld1"])
                x.w("tensor", "vec", x.c["vec"])
                x.w("tensor", "act", x.c["act"])
                for i in range(DC):
                    for ph in range(3):
                        lhs = xnh[i] if ph < 2 else xnl[i]
                        rhs_c = (i * 16) if ph != 1 else (i * 16 + 8)
                        x.op("tensor", "mm" if (i == DC - 1 and ph == 2) else None,
                             lambda e, i=i, ph=ph, lhs=lhs, rhs_c=rhs_c, sc=sc: e.matmul(
                                 ps_g[:, :], lhs[:, sc * 128:sc * 128 + 128], gws[:, rhs_c:rhs_c + 8],
                                 start=(i == 0 and ph == 0), stop=(i == DC - 1 and ph == 2)))
                x.w("vector", "mm", x.c["mm"])
                x.op("vector", None, lambda e: e.reduce_max(out=mrow[:, 0:1], in_=ps_g[:, :], axis=AX.X))
                x.op("vector", None, lambda e: e.tensor_scalar_mul(out=mrow[:, 1:2], in0=mrow[:, 0:1], scalar1=-1.0))
                x.op("vector", None, lambda e: e.tensor_scalar(out=l2t[:, :], in0=ps_g[:, :], scalar1=mrow[:, 0:1],
                                                               scalar2=None, op0=ALU.is_ge))
                x.op("vector", None, lambda e: e.scalar_tensor_tensor(out=l2t[:, :], in0=l2t[:, :], scalar=-1e30,
                                                                      in1=ps_g[:, :], op0=ALU.mult, op1=ALU.add))
                x.op("vector", "vec", lambda e: e.reduce_max(out=mrow[:, 2:3], in_=l2t[:, :], axis=AX.X))
                x.op("vector", "vec", lambda e: e.tensor_scalar(out=keep[:, :], in0=ps_g[:, :], scalar1=mrow[:, 2:3],
                                                                scalar2=None, op0=ALU.is_ge))
                x.w("scalar", "vec", x.c["vec"])
                x.op("scalar", "act", lambda e: e.activation(out=pex[:, :], in_=ps_g[:, :], func=AF.Exp,
                                                             bias=mrow[:, 1:2]))
                x.w("vector", "act", x.c["act"])
                x.op("vector", None, lambda e: e.tensor_mul(out=pex[:, :], in0=pex[:, :], in1=keep[:, :]))
                x.op("vector", None, lambda e: e.reduce_sum(out=mrow[:, 3:4], in_=pex[:, :], axis=AX.X))
                x.op("vector", None, lambda e: e.reciprocal(out=mrow[:, 3:4], in_=mrow[:, 3:4]))
                x.op("vector", "vec", lambda e: e.tensor_scalar(out=wts[:, :], in0=pex[:, :], scalar1=mrow[:, 3:4],
                                                                scalar2=None, op0=ALU.mult))
                x.w("tensor", "vec", x.c["vec"])
                x.op("tensor", "mm", lambda e: e.transpose(ps_t[0:8, 0:128], wts[:, :], idn[:, :]))
                x.w("vector", "mm", x.c["mm"])
                x.op("vector", "vec", lambda e, sc=sc: e.tensor_copy(out=wtsT[:, sc * 128:sc * 128 + 128], in_=ps_t[0:8, 0:128]))
            for i in range(DC):
                x.op("vector", "vec", lambda e, i=i: e.memset(acc[i][:, :], 0.0))
            acc_ready = x.c["vec"]
            for e_i in range(8):
                x.w("tensor", "vec", x.c["vec"])
                x.op("tensor", "mm", lambda e, e_i=e_i: e.matmul(
                    ps_w[:, :], oh_sb[:, e_i * 128:(e_i + 1) * 128], wtsT[:, :], start=True, stop=True))
                x.w("vector", "mm", x.c["mm"])
                x.op("vector", "vec", lambda e: e.tensor_copy(out=wb[:, :], in_=ps_w[:, :]))
                wb_ready = x.c["vec"]
                for g in range(4):
                    x.w("sync", "mm", x.c["mm"])
                    x.w("sync", "act", x.c["act"])
                    for i in range(DC):
                        x.dma("sync", "ld2", wfc[i][:, :], fcw[e_i, i * 128:(i + 1) * 128, g * 1024:(g + 1) * 1024])
                    for i in range(8):
                        x.dma("sync", "ld2", wpj[i][:, :], pjw[e_i, g * 1024 + i * 128:g * 1024 + (i + 1) * 128, :])
                    gtar = x.c["ld2"]
                    for hc in range(8):
                        x.w("tensor", "ld2", gtar)
                        if hc == 0:
                            x.w("tensor", "act", x.c["act"])
                        for i in range(DC):
                            x.op("tensor", "mm" if i == DC - 1 else None, lambda e, i=i, hc=hc: e.matmul(
                                ps_h[:, :], wfc[i][:, hc * 128:(hc + 1) * 128], xnh[i][:, :],
                                start=(i == 0), stop=(i == DC - 1)))
                        x.w("scalar", "mm", x.c["mm"])
                        x.op("scalar", "act", lambda e, hc=hc: e.activation(out=hsb[hc][:, :], in_=ps_h[:, :], func=AF.Gelu))
                        x.w("tensor", "act", x.c["act"])
                    for oc in range(8):
                        x.w("tensor", "vec", x.c["vec"])
                        for hc in range(8):
                            x.op("tensor", "mm" if hc == 7 else None, lambda e, hc=hc, oc=oc: e.matmul(
                                ps_o[:, :], wpj[hc][:, oc * 128:(oc + 1) * 128], hsb[hc][:, :],
                                start=(hc == 0), stop=(hc == 7)))
                        x.w("vector", "mm", x.c["mm"])
                        x.w("vector", "vec", wb_ready)
                        x.op("vector", None, lambda e: e.tensor_mul(out=tmp[:, :], in0=ps_o[:, :], in1=wb[:, :]))
                        x.op("vector", "vec", lambda e, oc=oc: e.tensor_add(out=acc[oc][:, :], in0=acc[oc][:, :], in1=tmp[:, :]))
            for i in range(DC):
                x.dma("sync", "ld", xs[:, :], xT_d[i * 128:(i + 1) * 128, t0:t0 + TS])
                x.dma("sync", "ld", ycs[:, :], yc_d[i * 128:(i + 1) * 128, t0:t0 + TS])
                x.w("vector", "ld", x.c["ld"])
                x.op("vector", None, lambda e, i=i: e.tensor_add(out=acc[i][:, :], in0=acc[i][:, :], in1=xs[:, :]))
                x.op("vector", "vec", lambda e, i=i: e.tensor_add(out=acc[i][:, :], in0=acc[i][:, :], in1=ycs[:, :]))
                x.w("gpsimd", "vec", x.c["vec"])
                x.dma("gpsimd", "st", x1_d[i * 128:(i + 1) * 128, t0:t0 + TS], acc[i][:, :])
                x.w("sync", "vec", x.c["vec"])

            def consume(x2, i, r_b, nm_b):
                x2.op("vector", None, lambda e: e.tensor_mul(out=xnf[:, :], in0=acc[i][:, :], in1=r_b))
                x2.op("vector", None, lambda e: e.tensor_add(out=xnf[:, :], in0=xnf[:, :], in1=nm_b))
                x2.op("vector", None, lambda e: e.tensor_copy(out=hi[:, :], in_=xnf[:, :]))
                x2.op("vector", "vec", lambda e: e.tensor_sub(out=lo[:, :], in0=xnf[:, :], in1=hi[:, :]))
                x2.w("gpsimd", "vec", x2.c["vec"])
                x2.dma("gpsimd", "st", oh_d[i * 128:(i + 1) * 128, t0:t0 + TS], hi[:, :])
                x2.dma("gpsimd", "st", ol_d[i * 128:(i + 1) * 128, t0:t0 + TS], lo[:, :])
                x2.w("vector", "st", x2.c["st"])
            emit_ln(x, lnk, [acc[i][:, :] for i in range(DC)], on, onT, 1e-5, consume)
        x.w("gpsimd", "st")
        x.w("sync", "st")
    return make_block(nc, body)


def build_pdproj():
    nc = new_nc()
    xnh_d = nc.dram_tensor("xnh", [D, T], BF16, kind="ExternalInput").ap()
    wq = nc.dram_tensor("wq", [D, D], BF16, kind="ExternalInput").ap()
    wk = nc.dram_tensor("wk", [D, D], BF16, kind="ExternalInput").ap()
    wv = nc.dram_tensor("wv", [D, D], BF16, kind="ExternalInput").ap()
    wg = nc.dram_tensor("wg", [D, D], BF16, kind="ExternalInput").ap()
    bg = nc.dram_tensor("bg", [D, 1], F32, kind="ExternalInput").ap()
    qfold_d = nc.dram_tensor("qfold", [8, 128, TS], F32, kind="ExternalInput").ap()
    kfold_d = nc.dram_tensor("kfold", [8, 128, TS], F32, kind="ExternalInput").ap()
    identb_d = nc.dram_tensor("cb_identb", [128, 128], BF16, kind="ExternalInput").ap()
    QT_d = nc.dram_tensor("QT", [D, T], BF16, kind="ExternalOutput").ap()
    KT_d = nc.dram_tensor("KT", [D, T], BF16, kind="ExternalOutput").ap()
    GT_d = nc.dram_tensor("GT", [D, T], BF16, kind="ExternalOutput").ap()
    VT_d = nc.dram_tensor("VT", [16, T, 64], BF16, kind="ExternalOutput").ap()

    xnh = [nc.alloc_sbuf_tensor(f"xnh{i}", [128, TS], BF16) for i in range(DC)]
    wsb = {n: [nc.alloc_sbuf_tensor(f"w{n}{i}", [128, D], BF16) for i in range(DC)] for n in "qkvg"}
    qf = [nc.alloc_sbuf_tensor(f"qf{i}", [128, TS], F32) for i in range(8)]
    kf = [nc.alloc_sbuf_tensor(f"kf{i}", [128, TS], F32) for i in range(8)]
    bgs = nc.alloc_sbuf_tensor("bgs", [128, 8], F32)
    idb = nc.alloc_sbuf_tensor("idb", [128, 128], BF16)
    ev = nc.alloc_sbuf_tensor("ev", [128, TS], BF16)
    evg = nc.alloc_sbuf_tensor("evg", [128, TS], BF16)
    vtT = nc.alloc_sbuf_tensor("vtT", [128, 128], BF16)
    ps = nc.alloc_psum_tensor("ps", [128, TS], F32)
    ps_tr = nc.alloc_psum_tensor("ps_tr", [128, 128], BF16)

    def body(x: Ctx):
        for n in "qkvg":
            w_d = {"q": wq, "k": wk, "v": wv, "g": wg}[n]
            for i in range(DC):
                x.dma("sync", "ld1", wsb[n][i][:, :], w_d[i * 128:(i + 1) * 128, :])
        for i in range(8):
            x.dma("sync", "ld1", qf[i][:, :], qfold_d[i, :, :])
            x.dma("sync", "ld1", kf[i][:, :], kfold_d[i, :, :])
        for i in range(DC):
            x.dma("sync", "ld1", bgs[:, i:i + 1], bg[i * 128:(i + 1) * 128, :])
        x.dma("sync", "ld1", idb[:, :], identb_d[:, :])
        wtar = x.c["ld1"]
        for ts in range(T // TS):
            t0 = ts * TS
            x.w("sync", "st", x.c["st"])
            x.w("sync", "vec", x.c["vec"])
            x.w("sync", "act", x.c["act"])
            for i in range(DC):
                x.dma("sync", "ld", xnh[i][:, :], xnh_d[i * 128:(i + 1) * 128, t0:t0 + TS])
            ldtar = x.c["ld"]
            x.w("vector", "ld1", wtar)
            x.w("scalar", "ld1", wtar)
            for n in "qkvg":
                for oc in range(8):
                    x.w("tensor", "ld", ldtar)
                    x.w("tensor", "ld1", wtar)
                    x.w("tensor", "vec", x.c["vec"])
                    x.w("tensor", "act", x.c["act"])
                    for i in range(DC):
                        x.op("tensor", "mm" if i == DC - 1 else None, lambda e, n=n, oc=oc, i=i: e.matmul(
                            ps[:, :], wsb[n][i][:, oc * 128:(oc + 1) * 128], xnh[i][:, :],
                            start=(i == 0), stop=(i == DC - 1)))
                    if n == "q":
                        x.w("vector", "mm", x.c["mm"])
                        x.op("vector", "vec", lambda e, oc=oc: e.tensor_mul(out=ev[:, :], in0=ps[:, :], in1=qf[oc][:, :]))
                        x.w("gpsimd", "vec", x.c["vec"])
                        x.dma("gpsimd", "st", QT_d[oc * 128:(oc + 1) * 128, t0:t0 + TS], ev[:, :])
                        x.w("vector", "st", x.c["st"])
                    elif n == "k":
                        x.w("vector", "mm", x.c["mm"])
                        x.op("vector", "vec", lambda e, oc=oc: e.tensor_mul(out=ev[:, :], in0=ps[:, :], in1=kf[oc][:, :]))
                        x.w("gpsimd", "vec", x.c["vec"])
                        x.dma("gpsimd", "st", KT_d[oc * 128:(oc + 1) * 128, t0:t0 + TS], ev[:, :])
                        x.w("vector", "st", x.c["st"])
                    elif n == "g":
                        x.w("scalar", "mm", x.c["mm"])
                        x.w("scalar", "vec", x.c["vec"])
                        x.op("scalar", "act", lambda e, oc=oc: e.activation(out=evg[:, :], in_=ps[:, :], func=AF.Silu,
                                                                            bias=bgs[:, oc:oc + 1]))
                        x.w("gpsimd", "act", x.c["act"])
                        x.dma("gpsimd", "st1", GT_d[oc * 128:(oc + 1) * 128, t0:t0 + TS], evg[:, :])
                        x.w("scalar", "st1", x.c["st1"])
                    else:
                        x.w("vector", "mm", x.c["mm"])
                        x.op("vector", "vec", lambda e: e.tensor_copy(out=ev[:, :], in_=ps[:, :]))
                        for c in range(4):
                            x.w("tensor", "vec", x.c["vec"])
                            x.op("tensor", "mm", lambda e, c=c: e.transpose(
                                ps_tr[:, :], ev[:, c * 128:(c + 1) * 128], idb[:, :]))
                            x.w("vector", "mm", x.c["mm"])
                            x.op("vector", "vec", lambda e: e.tensor_copy(out=vtT[:, :], in_=ps_tr[:, :]))
                            x.w("gpsimd", "vec", x.c["vec"])
                            h0, h1 = 2 * oc, 2 * oc + 1
                            base0 = h0 * (T * 64) + (t0 + c * 128) * 64
                            base1 = h1 * (T * 64) + (t0 + c * 128) * 64
                            x.dma("gpsimd", "st", bass.AP(VT_d.tensor, base0, [[64, 128], [1, 64]]), vtT[:, 0:64])
                            x.dma("gpsimd", "st", bass.AP(VT_d.tensor, base1, [[64, 128], [1, 64]]), vtT[:, 64:128])
                            x.w("vector", "st", x.c["st"])
                        x.w("vector", "st", x.c["st"])
        x.w("gpsimd", "st")
        x.w("gpsimd", "st1")
        x.w("sync", "st")
    return make_block(nc, body)


def build_pdcore():
    nc = new_nc()
    QT_d = nc.dram_tensor("QT", [D, T], BF16, kind="ExternalInput").ap()
    KT_d = nc.dram_tensor("KT", [D, T], BF16, kind="ExternalInput").ap()
    VT_d = nc.dram_tensor("VT", [16, T, 64], BF16, kind="ExternalInput").ap()
    cgt_d = nc.dram_tensor("cgt", [128, 8 * 144], F32, kind="ExternalInput").ap()
    msk_d = nc.dram_tensor("msk", [4, 128, TS], BF16, kind="ExternalInput").ap()
    RET_d = nc.dram_tensor("RET", [512, T], F32, kind="ExternalOutput").ap()
    qt = nc.alloc_sbuf_tensor("qt", [64, T], BF16)
    kt = nc.alloc_sbuf_tensor("kt", [64, T], BF16)
    vt = nc.alloc_sbuf_tensor("vt", [128, (T // 128) * 64], BF16)
    cgt = nc.alloc_sbuf_tensor("cgt_sb", [128, 8 * 144], F32)
    msk = [nc.alloc_sbuf_tensor(f"msk{i}", [128, TS], BF16) for i in range(4)]
    sT = [nc.alloc_sbuf_tensor(f"sT{i}", [128, TS], BF16) for i in range(2)]
    rsb = nc.alloc_sbuf_tensor("rsb", [64, TS], F32)
    ps_s = [nc.alloc_psum_tensor(f"ps_s{i}", [128, TS], F32) for i in range(2)]
    ps_r = nc.alloc_psum_tensor("ps_r", [64, TS], F32)

    def body(x: Ctx, hh0):
        x.dma("sync", "ld1", cgt[:, :], cgt_d[:, :])
        for i in range(4):
            x.dma("sync", "ld1", msk[i][:, :], msk_d[i, :, :])
        NG = T // TS
        NTJ = T // 128
        for hloc in range(8):
            h = hh0 + hloc
            x.w("sync", "mm", x.c["mm"])
            x.dma("sync", "ld", qt[:, :], QT_d[h * 64:(h + 1) * 64, :])
            x.dma("sync", "ld", kt[:, :], KT_d[h * 64:(h + 1) * 64, :])
            x.dma("sync", "ld", vt[:, :], bass.AP(VT_d.tensor, h * (T * 64), [[64, 128], [128 * 64, NTJ], [1, 64]]))
            ldtar = x.c["ld"]
            x.w("vector", "ld1", x.c["ld1"])
            kcnt = 0
            for G in range(NG):
                njt = min(4 * G + 3, NTJ - 1) + 1
                for TJ in range(njt):
                    par = kcnt % 2
                    x.w("tensor", "ld", ldtar)
                    x.w("tensor", "vec", x.c["vec"])
                    x.op("tensor", "mm", lambda e, G=G, TJ=TJ, par=par: e.matmul(
                        ps_s[par][:, :], kt[:, TJ * 128:(TJ + 1) * 128], qt[:, G * TS:(G + 1) * TS],
                        start=True, stop=True))
                    col = hloc * 144 + kcnt
                    x.w("vector", "mm", x.c["mm"])
                    x.op("vector", "vec", lambda e, par=par, col=col: e.tensor_scalar(
                        out=sT[par][:, :], in0=ps_s[par][:, :], scalar1=cgt[:, col:col + 1],
                        scalar2=None, op0=ALU.mult))
                    if TJ >= 4 * G:
                        x.op("vector", "vec", lambda e, par=par, r=TJ - 4 * G: e.tensor_mul(
                            out=sT[par][:, :], in0=sT[par][:, :], in1=msk[r][:, :]))
                    x.w("tensor", "vec", x.c["vec"])
                    x.op("tensor", "mm", lambda e, TJ=TJ, par=par: e.matmul(
                        ps_r[:, :], vt[:, TJ * 64:(TJ + 1) * 64], sT[par][:, :],
                        start=(TJ == 0), stop=(TJ == njt - 1)))
                    kcnt += 1
                x.w("vector", "mm", x.c["mm"])
                x.w("vector", "st", x.c["st"])
                x.op("vector", "vec", lambda e: e.tensor_copy(out=rsb[:, :], in_=ps_r[:, :]))
                x.w("gpsimd", "vec", x.c["vec"])
                x.dma("gpsimd", "st", RET_d[hloc * 64:(hloc + 1) * 64, G * TS:(G + 1) * TS], rsb[:, :])
                x.w("tensor", "vec", x.c["vec"])
        x.w("gpsimd", "st")
        x.w("sync", "st")
    return nc, body


def kcnt_of(G, TJ):
    k = 0
    for g in range(G):
        k += min(4 * g + 3, 31) + 1
    return k + TJ


def build_pdcore_block(hh0):
    nc, body = build_pdcore()
    return make_block(nc, lambda x: body(x, hh0))


def build_pdln():
    nc = new_nc()
    RET_d = nc.dram_tensor("RET", [D, T], F32, kind="ExternalInput").ap()
    GT_d = nc.dram_tensor("GT", [D, T], BF16, kind="ExternalInput").ap()
    woo = nc.dram_tensor("woo", [D, D], BF16, kind="ExternalInput").ap()
    bo = nc.dram_tensor("bo", [D, 1], F32, kind="ExternalInput").ap()
    lng = nc.dram_tensor("lng", [D, 1], F32, kind="ExternalInput").ap()
    lnb = nc.dram_tensor("lnb", [D, 1], F32, kind="ExternalInput").ap()
    ones = nc.dram_tensor("cb_ones", [128, 1], F32, kind="ExternalInput").ap()
    onesT = nc.dram_tensor("cb_onesT", [1, 128], F32, kind="ExternalInput").ap()
    A1_d = nc.dram_tensor("A1", [D, T], F32, kind="ExternalOutput").ap()

    rt = [nc.alloc_sbuf_tensor(f"rt{i}", [128, TS], F32) for i in range(DC)]
    gt = nc.alloc_sbuf_tensor("gt", [128, TS], BF16)
    gg = [nc.alloc_sbuf_tensor(f"gg{i}", [128, TS], BF16) for i in range(DC)]
    wsb = [nc.alloc_sbuf_tensor(f"wsb{i}", [128, D], BF16) for i in range(DC)]
    prm = nc.alloc_sbuf_tensor("prm", [128, 4 * DC], F32)
    on = nc.alloc_sbuf_tensor("on", [128, 1], F32)
    onT = nc.alloc_sbuf_tensor("onT", [1, 128], F32)
    xnf = nc.alloc_sbuf_tensor("xnf", [128, TS], F32)
    ev = nc.alloc_sbuf_tensor("ev", [128, TS], F32)
    lnk = ln_alloc(nc, TS)
    ps = nc.alloc_psum_tensor("ps", [128, TS], F32)

    def body(x: Ctx):
        x.dma("sync", "ld1", on[:, :], ones[:, :])
        x.dma("sync", "ld1", onT[:, :], onesT[:, :])
        for i in range(DC):
            x.dma("sync", "ld1", wsb[i][:, :], woo[i * 128:(i + 1) * 128, :])
            x.dma("sync", "ld1", prm[:, i:i + 1], bo[i * 128:(i + 1) * 128, :])
            x.dma("sync", "ld1", prm[:, DC + i:DC + i + 1], lng[i * 128:(i + 1) * 128, :])
            x.dma("sync", "ld1", prm[:, 2 * DC + i:2 * DC + i + 1], lnb[i * 128:(i + 1) * 128, :])
        for ts in range(T // TS):
            t0 = ts * TS
            x.w("sync", "st", x.c["st"])
            x.w("sync", "mm", x.c["mm"])
            for i in range(DC):
                x.dma("sync", "ld", rt[i][:, :], RET_d[i * 128:(i + 1) * 128, t0:t0 + TS])
            ldtar = x.c["ld"]
            x.w("tensor", "ld", ldtar)
            x.w("tensor", "ld1", x.c["ld1"])
            x.w("scalar", "ld", ldtar)
            x.w("vector", "ld1", x.c["ld1"])

            def consume(x2, i, r_b, nm_b):
                x2.op("vector", None, lambda e, i=i: e.tensor_mul(out=xnf[:, :], in0=rt[i][:, :], in1=r_b))
                x2.op("vector", "vec", lambda e, i=i: e.tensor_add(out=xnf[:, :], in0=xnf[:, :], in1=nm_b))
                x2.op("vector", "vec", lambda e, i=i: e.tensor_scalar(
                    out=ev[:, :], in0=xnf[:, :], scalar1=prm[:, DC + i:DC + i + 1],
                    scalar2=prm[:, 2 * DC + i:2 * DC + i + 1], op0=ALU.mult, op1=ALU.add))
                x2.dma("sync", "ld2", gt[:, :], GT_d[i * 128:(i + 1) * 128, t0:t0 + TS])
                x2.w("vector", "ld2", x2.c["ld2"])
                x2.op("vector", "vec", lambda e, i=i: e.tensor_mul(out=gg[i][:, :], in0=ev[:, :], in1=gt[:, :]))
                x2.w("sync", "vec", x2.c["vec"])

            emit_ln(x, lnk, [rt[i][:, :] for i in range(DC)], on, onT, 1e-3, consume)
            ggtar = x.c["vec"]
            for oc in range(8):
                x.w("tensor", "vec", x.c["vec"])
                for i in range(DC):
                    x.op("tensor", "mm" if i == DC - 1 else None, lambda e, oc=oc, i=i: e.matmul(
                        ps[:, :], wsb[i][:, oc * 128:(oc + 1) * 128], gg[i][:, :],
                        start=(i == 0), stop=(i == DC - 1)))
                x.w("vector", "mm", x.c["mm"])
                x.w("vector", "st", x.c["st"])
                x.op("vector", "vec", lambda e, oc=oc: e.tensor_scalar(
                    out=ev[:, :], in0=ps[:, :], scalar1=prm[:, oc:oc + 1], scalar2=None, op0=ALU.add))
                x.w("gpsimd", "vec", x.c["vec"])
                x.dma("gpsimd", "st", A1_d[oc * 128:(oc + 1) * 128, t0:t0 + TS], ev[:, :])
        x.w("gpsimd", "st")
        x.w("sync", "st")
    return make_block(nc, body)


def build_peqkv():
    nc = new_nc()
    h0_d = nc.dram_tensor("h0", [D, T], BF16, kind="ExternalInput").ap()
    wqkv = nc.dram_tensor("wqkv", [D, 1536], BF16, kind="ExternalInput").ap()
    qs_d = nc.dram_tensor("qs", [128, 1], F32, kind="ExternalInput").ap()
    ks_d = nc.dram_tensor("ks", [128, 1], F32, kind="ExternalInput").ap()
    csq_d = nc.dram_tensor("csq", [128, T], F32, kind="ExternalInput").ap()
    ssq_d = nc.dram_tensor("ssq", [128, T], F32, kind="ExternalInput").ap()
    csk_d = nc.dram_tensor("csk", [128, T], F32, kind="ExternalInput").ap()
    ssk_d = nc.dram_tensor("ssk", [128, T], F32, kind="ExternalInput").ap()
    hm2_d = nc.dram_tensor("cb_hm2", [128, 2], F32, kind="ExternalInput").ap()
    hm2T_d = nc.dram_tensor("cb_hm2T", [2, 128], F32, kind="ExternalInput").ap()
    identb_d = nc.dram_tensor("cb_identb", [128, 128], BF16, kind="ExternalInput").ap()
    QR_d = nc.dram_tensor("QR", [512, T], BF16, kind="ExternalOutput").ap()
    KR_d = nc.dram_tensor("KR", [512, T], BF16, kind="ExternalOutput").ap()
    VR_d = nc.dram_tensor("VR", [8, T, 64], BF16, kind="ExternalOutput").ap()

    h0 = [nc.alloc_sbuf_tensor(f"h0{i}", [128, TS], BF16) for i in range(DC)]
    wsb = [nc.alloc_sbuf_tensor(f"wsb{i}", [128, 1536], BF16) for i in range(DC)]
    tabs = {n: nc.alloc_sbuf_tensor(f"tab{n}", [128, TS], F32) for n in ("csq", "ssq", "csk", "ssk")}
    qsks = nc.alloc_sbuf_tensor("qsks", [128, 2], F32)
    hm2 = nc.alloc_sbuf_tensor("hm2", [128, 2], F32)
    hm2T = nc.alloc_sbuf_tensor("hm2T", [2, 128], F32)
    idb = nc.alloc_sbuf_tensor("idb", [128, 128], BF16)
    qsb = nc.alloc_sbuf_tensor("qsb", [128, TS], F32)
    sqt = nc.alloc_sbuf_tensor("sqt", [128, TS], F32)
    ssl = nc.alloc_sbuf_tensor("ssl", [2, 2 * TS], F32)
    rr = nc.alloc_sbuf_tensor("rr", [2, TS], F32)
    rh = nc.alloc_sbuf_tensor("rh", [128, TS], F32)
    t1 = nc.alloc_sbuf_tensor("t1", [128, TS], F32)
    t2 = nc.alloc_sbuf_tensor("t2", [128, TS], F32)
    qf = nc.alloc_sbuf_tensor("qf", [128, TS], BF16)
    ev = nc.alloc_sbuf_tensor("ev", [128, TS], BF16)
    vtT = nc.alloc_sbuf_tensor("vtT", [128, 128], BF16)
    ps = nc.alloc_psum_tensor("ps", [128, TS], F32)
    ps_ss = nc.alloc_psum_tensor("ps_ss", [2, TS], F32)
    ps_bc = nc.alloc_psum_tensor("ps_bc", [128, TS], F32)
    ps_tr = nc.alloc_psum_tensor("ps_tr", [128, 128], BF16)

    def body(x: Ctx):
        for i in range(DC):
            x.dma("sync", "ld1", wsb[i][:, :], wqkv[i * 128:(i + 1) * 128, :])
        x.dma("sync", "ld1", qsks[:, 0:1], qs_d[:, :])
        x.dma("sync", "ld1", qsks[:, 1:2], ks_d[:, :])
        x.dma("sync", "ld1", hm2[:, :], hm2_d[:, :])
        x.dma("sync", "ld1", hm2T[:, :], hm2T_d[:, :])
        x.dma("sync", "ld1", idb[:, :], identb_d[:, :])
        wtar = x.c["ld1"]
        for ts in range(T // TS):
            t0 = ts * TS
            x.w("sync", "st", x.c["st"])
            x.w("sync", "mm", x.c["mm"])
            x.w("sync", "vec", x.c["vec"])
            for i in range(DC):
                x.dma("sync", "ld", h0[i][:, :], h0_d[i * 128:(i + 1) * 128, t0:t0 + TS])
            for n in tabs:
                src = {"csq": csq_d, "ssq": ssq_d, "csk": csk_d, "ssk": ssk_d}[n]
                x.dma("sync", "ld", tabs[n][:, :], src[:, t0:t0 + TS])
            ldtar = x.c["ld"]
            x.w("vector", "ld", ldtar)
            x.w("vector", "ld1", wtar)
            for oc in range(12):
                x.w("tensor", "ld", ldtar)
                x.w("tensor", "ld1", wtar)
                x.w("tensor", "vec", x.c["vec"])
                x.w("tensor", "act", x.c["act"])
                for i in range(DC):
                    x.op("tensor", "mm" if i == DC - 1 else None, lambda e, oc=oc, i=i: e.matmul(
                        ps[:, :], wsb[i][:, oc * 128:(oc + 1) * 128], h0[i][:, :],
                        start=(i == 0), stop=(i == DC - 1)))
                if oc >= 8:
                    x.w("vector", "mm", x.c["mm"])
                    x.op("vector", "vec", lambda e: e.tensor_copy(out=ev[:, :], in_=ps[:, :]))
                    for c in range(4):
                        x.w("tensor", "vec", x.c["vec"])
                        x.op("tensor", "mm", lambda e, c=c: e.transpose(
                            ps_tr[:, :], ev[:, c * 128:(c + 1) * 128], idb[:, :]))
                        x.w("vector", "mm", x.c["mm"])
                        x.op("vector", "vec", lambda e: e.tensor_copy(out=vtT[:, :], in_=ps_tr[:, :]))
                        x.w("gpsimd", "vec", x.c["vec"])
                        voc = oc - 8
                        h0i, h1i = 2 * voc, 2 * voc + 1
                        b0 = h0i * (T * 64) + (t0 + c * 128) * 64
                        b1 = h1i * (T * 64) + (t0 + c * 128) * 64
                        x.dma("gpsimd", "st", bass.AP(VR_d.tensor, b0, [[64, 128], [1, 64]]), vtT[:, 0:64])
                        x.dma("gpsimd", "st", bass.AP(VR_d.tensor, b1, [[64, 128], [1, 64]]), vtT[:, 64:128])
                        x.w("vector", "st", x.c["st"])
                    x.w("vector", "st", x.c["st"])
                else:
                    isq = oc < 4
                    cs = tabs["csq" if isq else "csk"]
                    ss = tabs["ssq" if isq else "ssk"]
                    scol = 0 if isq else 1
                    x.w("vector", "mm", x.c["mm"])
                    x.op("vector", None, lambda e: e.tensor_copy(out=qsb[:, :], in_=ps[:, :]))
                    x.op("vector", "vec", lambda e: e.tensor_mul(out=sqt[:, :], in0=qsb[:, :], in1=qsb[:, :]))
                    x.w("tensor", "vec", x.c["vec"])
                    x.op("tensor", "mm", lambda e: e.matmul(ps_ss[:, :], hm2[:, :], sqt[:, :], start=True, stop=True))
                    x.w("vector", "mm", x.c["mm"])
                    x.op("vector", "vec", lambda e: e.tensor_copy(out=ssl[:, 0:TS], in_=ps_ss[:, :]))
                    x.w("scalar", "vec", x.c["vec"])
                    x.op("scalar", "act", lambda e: e.activation(out=ssl[:, TS:2 * TS], in_=ssl[:, 0:TS], func=AF.Sqrt))
                    x.w("vector", "act", x.c["act"])
                    x.op("vector", "vec", lambda e: e.reciprocal(out=rr[:, :], in_=ssl[:, TS:2 * TS]))
                    x.w("tensor", "vec", x.c["vec"])
                    x.op("tensor", "mm", lambda e: e.matmul(ps_bc[:, :], hm2T[:, :], rr[:, :], start=True, stop=True))
                    x.w("vector", "mm", x.c["mm"])
                    x.op("vector", None, lambda e: e.tensor_mul(out=qsb[:, :], in0=qsb[:, :], in1=ps_bc[:, :]))
                    x.op("vector", "vec", lambda e, scol=scol: e.tensor_scalar(
                        out=qsb[:, :], in0=qsb[:, :], scalar1=qsks[:, scol:scol + 1], scalar2=None, op0=ALU.mult))
                    x.w("gpsimd", "vec", x.c["vec"])
                    for (d0, s0) in ((0, 32), (32, 0), (64, 96), (96, 64)):
                        x.dma("gpsimd", "gp", rh[d0:d0 + 32, :], qsb[s0:s0 + 32, :])
                    x.w("vector", "gp", x.c["gp"])
                    x.op("vector", None, lambda e, cs=cs: e.tensor_mul(out=t1[:, :], in0=qsb[:, :], in1=cs[:, :]))
                    x.op("vector", None, lambda e, ss=ss: e.tensor_mul(out=t2[:, :], in0=rh[:, :], in1=ss[:, :]))
                    x.op("vector", "vec", lambda e: e.tensor_add(out=qf[:, :], in0=t1[:, :], in1=t2[:, :]))
                    x.w("gpsimd", "vec", x.c["vec"])
                    dst = QR_d if isq else KR_d
                    roc = oc if isq else oc - 4
                    x.dma("gpsimd", "st", dst[roc * 128:(roc + 1) * 128, t0:t0 + TS], qf[:, :])
                    x.w("vector", "st", x.c["st"])
        x.w("gpsimd", "st")
        x.w("sync", "st")
    return make_block(nc, body)


def build_peattn():
    nc = new_nc()
    QR_d = nc.dram_tensor("QR", [512, T], BF16, kind="ExternalInput").ap()
    KR_d = nc.dram_tensor("KR", [512, T], BF16, kind="ExternalInput").ap()
    VR_d = nc.dram_tensor("VR", [8, T, 64], BF16, kind="ExternalInput").ap()
    am_d = nc.dram_tensor("amask", [128, 640], F32, kind="ExternalInput").ap()
    identb_d = nc.dram_tensor("cb_identb", [128, 128], BF16, kind="ExternalInput").ap()
    OT_d = nc.dram_tensor("OT", [512, T], BF16, kind="ExternalOutput").ap()

    qt = nc.alloc_sbuf_tensor("qt", [64, 128], BF16)
    kt = nc.alloc_sbuf_tensor("kt", [64, 640], BF16)
    vt = nc.alloc_sbuf_tensor("vt", [128, 5 * 64], BF16)
    ams = nc.alloc_sbuf_tensor("ams", [128, 640], F32)
    idb = nc.alloc_sbuf_tensor("idb", [128, 128], BF16)
    sim = nc.alloc_sbuf_tensor("sim", [128, 640], F32)
    esb = nc.alloc_sbuf_tensor("esb", [128, 640], F32)
    aw = nc.alloc_sbuf_tensor("aw", [128, 640], BF16)
    awT = nc.alloc_sbuf_tensor("awT", [128, 640], BF16)
    mx = nc.alloc_sbuf_tensor("mx", [128, 4], F32)
    ob = nc.alloc_sbuf_tensor("ob", [64, 128], BF16)
    ps_sim = nc.alloc_psum_tensor("ps_sim", [128, 640], F32)
    ps_tr = nc.alloc_psum_tensor("ps_tr", [128, 128], BF16)
    ps_o = nc.alloc_psum_tensor("ps_o", [64, 128], F32)

    def body(x: Ctx):
        x.dma("sync", "ld1", ams[:, :], am_d[:, :])
        x.dma("sync", "ld1", idb[:, :], identb_d[:, :])
        for h in range(8):
            for ic in range(T // 128):
                nj = min(ic + 1, 5)
                jsp = nj * 128
                j0 = max(0, (ic - 4) * 128)
                moff = (5 - nj) * 128
                x.w("sync", "mm", x.c["mm"])
                x.w("sync", "vec", x.c["vec"])
                x.dma("sync", "ld", qt[:, :], QR_d[h * 64:(h + 1) * 64, ic * 128:(ic + 1) * 128])
                x.dma("sync", "ld", kt[:, 0:jsp], KR_d[h * 64:(h + 1) * 64, j0:j0 + jsp])
                x.dma("sync", "ld", vt[:, 0:nj * 64],
                      bass.AP(VR_d.tensor, h * (T * 64) + j0 * 64, [[64, 128], [128 * 64, nj], [1, 64]]))
                ldtar = x.c["ld"]
                x.w("tensor", "ld", ldtar)
                x.w("tensor", "ld1", x.c["ld1"])
                x.w("vector", "ld1", x.c["ld1"])
                x.w("tensor", "vec", x.c["vec"])
                for c0 in range(0, jsp, TS):
                    cw = min(TS, jsp - c0)
                    x.op("tensor", "mm" if c0 + cw == jsp else None, lambda e, c0=c0, cw=cw: e.matmul(
                        ps_sim[:, c0:c0 + cw], qt[:, :], kt[:, c0:c0 + cw], start=True, stop=True))
                x.w("vector", "mm", x.c["mm"])
                x.op("vector", None, lambda e, jsp=jsp, moff=moff: e.scalar_tensor_tensor(
                    out=sim[:, 0:jsp], in0=ps_sim[:, 0:jsp], scalar=8.0, in1=ams[:, moff:moff + jsp],
                    op0=ALU.mult, op1=ALU.add))
                x.op("vector", None, lambda e, jsp=jsp: e.reduce_max(out=mx[:, 0:1], in_=sim[:, 0:jsp], axis=AX.X))
                x.op("vector", "vec", lambda e: e.tensor_scalar_mul(out=mx[:, 1:2], in0=mx[:, 0:1], scalar1=-1.0))
                x.w("scalar", "vec", x.c["vec"])
                x.op("scalar", "act", lambda e, jsp=jsp: e.activation(out=esb[:, 0:jsp], in_=sim[:, 0:jsp],
                                                                      func=AF.Exp, bias=mx[:, 1:2]))
                x.w("vector", "act", x.c["act"])
                x.op("vector", None, lambda e, jsp=jsp: e.reduce_sum(out=mx[:, 2:3], in_=esb[:, 0:jsp], axis=AX.X))
                x.op("vector", None, lambda e: e.reciprocal(out=mx[:, 3:4], in_=mx[:, 2:3]))
                x.op("vector", "vec", lambda e, jsp=jsp: e.tensor_scalar(
                    out=aw[:, 0:jsp], in0=esb[:, 0:jsp], scalar1=mx[:, 3:4], scalar2=None, op0=ALU.mult))
                for c in range(nj):
                    x.w("tensor", "vec", x.c["vec"])
                    x.op("tensor", "mm", lambda e, c=c: e.transpose(ps_tr[:, :], aw[:, c * 128:(c + 1) * 128], idb[:, :]))
                    x.w("vector", "mm", x.c["mm"])
                    x.op("vector", "vec", lambda e, c=c: e.tensor_copy(out=awT[:, c * 128:(c + 1) * 128], in_=ps_tr[:, :]))
                x.w("tensor", "vec", x.c["vec"])
                for c in range(nj):
                    x.op("tensor", "mm" if c == nj - 1 else None, lambda e, c=c: e.matmul(
                        ps_o[:, :], vt[:, c * 64:(c + 1) * 64], awT[:, c * 128:(c + 1) * 128],
                        start=(c == 0), stop=(c == nj - 1)))
                x.w("vector", "mm", x.c["mm"])
                x.w("vector", "st", x.c["st"])
                x.op("vector", "vec", lambda e: e.tensor_copy(out=ob[:, :], in_=ps_o[:, :]))
                x.w("gpsimd", "vec", x.c["vec"])
                x.dma("gpsimd", "st", OT_d[h * 64:(h + 1) * 64, ic * 128:(ic + 1) * 128], ob[:, :])
        x.w("gpsimd", "st")
        x.w("sync", "st")
    return make_block(nc, body)


def build_pewo():
    nc = new_nc()
    OT_d = nc.dram_tensor("OT", [512, T], BF16, kind="ExternalInput").ap()
    wo_d = nc.dram_tensor("wo", [512, D], BF16, kind="ExternalInput").ap()
    rh_d = nc.dram_tensor("resh", [D, T], BF16, kind="ExternalInput").ap()
    rl_d = nc.dram_tensor("resl", [D, T], BF16, kind="ExternalInput").ap()
    XA_d = nc.dram_tensor("XA", [D, T], F32, kind="ExternalOutput").ap()

    ot = [nc.alloc_sbuf_tensor(f"ot{i}", [128, TS], BF16) for i in range(4)]
    wsb = [nc.alloc_sbuf_tensor(f"wsb{i}", [128, D], BF16) for i in range(4)]
    rhs = nc.alloc_sbuf_tensor("rhs", [128, TS], BF16)
    rls = nc.alloc_sbuf_tensor("rls", [128, TS], BF16)
    xa = nc.alloc_sbuf_tensor("xa", [128, TS], F32)
    ps = nc.alloc_psum_tensor("ps", [128, TS], F32)

    def body(x: Ctx):
        for i in range(4):
            x.dma("sync", "ld1", wsb[i][:, :], wo_d[i * 128:(i + 1) * 128, :])
        for ts in range(T // TS):
            t0 = ts * TS
            x.w("sync", "mm", x.c["mm"])
            for i in range(4):
                x.dma("sync", "ld", ot[i][:, :], OT_d[i * 128:(i + 1) * 128, t0:t0 + TS])
            ldtar = x.c["ld"]
            for oc in range(8):
                x.w("tensor", "ld", ldtar)
                x.w("tensor", "ld1", x.c["ld1"])
                x.w("tensor", "vec", x.c["vec"])
                for i in range(4):
                    x.op("tensor", "mm" if i == 3 else None, lambda e, oc=oc, i=i: e.matmul(
                        ps[:, :], wsb[i][:, oc * 128:(oc + 1) * 128], ot[i][:, :],
                        start=(i == 0), stop=(i == 3)))
                x.w("sync", "vec", x.c["vec"])
                x.dma("sync", "ld2", rhs[:, :], rh_d[oc * 128:(oc + 1) * 128, t0:t0 + TS])
                x.dma("sync", "ld2", rls[:, :], rl_d[oc * 128:(oc + 1) * 128, t0:t0 + TS])
                x.w("vector", "ld2", x.c["ld2"])
                x.w("vector", "mm", x.c["mm"])
                x.w("vector", "st", x.c["st"])
                x.op("vector", None, lambda e: e.tensor_add(out=xa[:, :], in0=ps[:, :], in1=rhs[:, :]))
                x.op("vector", "vec", lambda e: e.tensor_add(out=xa[:, :], in0=xa[:, :], in1=rls[:, :]))
                x.w("gpsimd", "vec", x.c["vec"])
                x.dma("gpsimd", "st", XA_d[oc * 128:(oc + 1) * 128, t0:t0 + TS], xa[:, :])
        x.w("gpsimd", "st")
        x.w("sync", "st")
    return make_block(nc, body)


def build_pe2(fin):
    nc = new_nc()
    XR_d = nc.dram_tensor("XR", [D, T], F32, kind="ExternalInput").ap()
    H1_d = nc.dram_tensor("H1", [D, T], F32, kind="ExternalInput").ap()
    w1_d = nc.dram_tensor("w1", [D, 5460], BF16, kind="ExternalInput").ap()
    w2_d = nc.dram_tensor("w2", [2730, D], BF16, kind="ExternalInput").ap()
    b1a_d = nc.dram_tensor("b1a", [128, 22], F32, kind="ExternalInput").ap()
    b1g_d = nc.dram_tensor("b1g", [128, 22], F32, kind="ExternalInput").ap()
    b2_d = nc.dram_tensor("b2", [D, 1], F32, kind="ExternalInput").ap()
    lg_d = nc.dram_tensor("lg", [D, 1], F32, kind="ExternalInput").ap()
    lb_d = nc.dram_tensor("lb", [D, 1], F32, kind="ExternalInput").ap()
    ones = nc.dram_tensor("cb_ones", [128, 1], F32, kind="ExternalInput").ap()
    onesT = nc.dram_tensor("cb_onesT", [1, 128], F32, kind="ExternalInput").ap()
    if fin:
        X1_d = nc.dram_tensor("X1S", [D, T], F32, kind="ExternalInput").ap()
        A1_d = nc.dram_tensor("A1S", [D, T], F32, kind="ExternalInput").ap()
    Z_d = nc.dram_tensor("Z", [D, T], F32, kind="ExternalOutput").ap()
    if not fin:
        Zh_d = nc.dram_tensor("Zh", [D, T], BF16, kind="ExternalOutput").ap()
        Zl_d = nc.dram_tensor("Zl", [D, T], BF16, kind="ExternalOutput").ap()

    h1 = [nc.alloc_sbuf_tensor(f"h1{i}", [128, TS], F32) for i in range(DC)]
    lnb_sb = [nc.alloc_sbuf_tensor(f"lnb{i}", [128, TS], BF16) for i in range(DC)]
    hsb = [nc.alloc_sbuf_tensor(f"hsb{i}", [128, TS], BF16) for i in range(22)]
    w1s = [nc.alloc_sbuf_tensor(f"w1s{i}", [128, 5460], BF16) for i in range(DC)]
    w2s = [nc.alloc_sbuf_tensor(f"w2s{i}", [128, D], BF16) for i in range(22)]
    prm = nc.alloc_sbuf_tensor("prm", [128, 2 * DC + 44 + DC], F32)
    on = nc.alloc_sbuf_tensor("on", [128, 1], F32)
    onT = nc.alloc_sbuf_tensor("onT", [1, 128], F32)
    xnf = nc.alloc_sbuf_tensor("xnf", [128, TS], F32)
    asb = nc.alloc_sbuf_tensor("asb", [128, TS], F32)
    gsb = nc.alloc_sbuf_tensor("gsb", [128, TS], F32)
    xr = nc.alloc_sbuf_tensor("xr", [128, TS], F32)
    zz = nc.alloc_sbuf_tensor("zz", [128, TS], F32)
    zh = nc.alloc_sbuf_tensor("zh", [128, TS], BF16)
    zl = nc.alloc_sbuf_tensor("zl", [128, TS], BF16)
    lnk = ln_alloc(nc, TS)
    ps_a = nc.alloc_psum_tensor("ps_a", [128, TS], F32)
    ps_g2 = nc.alloc_psum_tensor("ps_g2", [128, TS], F32)
    ps_z = nc.alloc_psum_tensor("ps_z", [128, TS], F32)

    CH = [min(128, 2730 - j * 128) for j in range(22)]

    def body(x: Ctx):
        x.dma("sync", "ld1", on[:, :], ones[:, :])
        x.dma("sync", "ld1", onT[:, :], onesT[:, :])
        for i in range(DC):
            x.dma("sync", "ld1", w1s[i][:, :], w1_d[i * 128:(i + 1) * 128, :])
            x.dma("sync", "ld1", prm[:, i:i + 1], lg_d[i * 128:(i + 1) * 128, :])
            x.dma("sync", "ld1", prm[:, DC + i:DC + i + 1], lb_d[i * 128:(i + 1) * 128, :])
            x.dma("sync", "ld1", prm[:, 2 * DC + 44 + i:2 * DC + 44 + i + 1], b2_d[i * 128:(i + 1) * 128, :])
        x.dma("sync", "ld1", prm[:, 2 * DC:2 * DC + 22], b1a_d[:, :])
        x.dma("sync", "ld1", prm[:, 2 * DC + 22:2 * DC + 44], b1g_d[:, :])
        for j in range(22):
            x.dma("sync", "ld1", w2s[j][:CH[j], :], w2_d[j * 128:j * 128 + CH[j], :])
        for ts in range(T // TS):
            t0 = ts * TS
            x.w("sync", "st", x.c["st"])
            x.w("sync", "mm", x.c["mm"])
            x.w("sync", "act", x.c["act"])
            for i in range(DC):
                x.dma("sync", "ld", h1[i][:, :], H1_d[i * 128:(i + 1) * 128, t0:t0 + TS])
            ldtar = x.c["ld"]
            x.w("tensor", "ld", ldtar)
            x.w("tensor", "ld1", x.c["ld1"])
            x.w("scalar", "ld", ldtar)
            x.w("scalar", "ld1", x.c["ld1"])
            x.w("vector", "ld", ldtar)
            x.w("vector", "ld1", x.c["ld1"])

            def consume(x2, i, r_b, nm_b):
                x2.op("vector", None, lambda e, i=i: e.tensor_mul(out=xnf[:, :], in0=h1[i][:, :], in1=r_b))
                x2.op("vector", None, lambda e: e.tensor_add(out=xnf[:, :], in0=xnf[:, :], in1=nm_b))
                x2.op("vector", "vec", lambda e, i=i: e.tensor_scalar(
                    out=lnb_sb[i][:, :], in0=xnf[:, :], scalar1=prm[:, i:i + 1],
                    scalar2=prm[:, DC + i:DC + i + 1], op0=ALU.mult, op1=ALU.add))

            emit_ln(x, lnk, [h1[i][:, :] for i in range(DC)], on, onT, 1e-5, consume)
            lntar = x.c["act"]
            for j in range(22):
                cj = CH[j]
                x.w("tensor", "act", x.c["act"])
                x.w("tensor", "vec", x.c["vec"])
                for i in range(DC):
                    x.op("tensor", "mm" if i == DC - 1 else None, lambda e, j=j, i=i, cj=cj: e.matmul(
                        ps_a[:cj, :], w1s[i][:, j * 128:j * 128 + cj], lnb_sb[i][:, :],
                        start=(i == 0), stop=(i == DC - 1)))
                for i in range(DC):
                    x.op("tensor", "mm" if i == DC - 1 else None, lambda e, j=j, i=i, cj=cj: e.matmul(
                        ps_g2[:cj, :], w1s[i][:, 2730 + j * 128:2730 + j * 128 + cj], lnb_sb[i][:, :],
                        start=(i == 0), stop=(i == DC - 1)))
                x.w("vector", "mm", x.c["mm"])
                x.op("vector", "vec", lambda e, j=j, cj=cj: e.tensor_scalar(
                    out=asb[:cj, :], in0=ps_a[:cj, :], scalar1=prm[0:cj, 2 * DC + j:2 * DC + j + 1],
                    scalar2=None, op0=ALU.add))
                x.w("scalar", "mm", x.c["mm"])
                x.op("scalar", "act", lambda e, j=j, cj=cj: e.activation(out=gsb[:cj, :], in_=ps_g2[:cj, :], func=AF.Gelu,
                                                                         bias=prm[0:cj, 2 * DC + 22 + j:2 * DC + 22 + j + 1]))
                x.w("vector", "act", x.c["act"])
                x.op("vector", "vec", lambda e, j=j, cj=cj: e.tensor_mul(out=hsb[j][:cj, :], in0=asb[:cj, :], in1=gsb[:cj, :]))
                x.w("scalar", "vec", x.c["vec"])
            htar = x.c["vec"]
            for oc in range(8):
                x.w("tensor", "vec", x.c["vec"])
                for j in range(22):
                    x.op("tensor", "mm" if j == 21 else None, lambda e, oc=oc, j=j: e.matmul(
                        ps_z[:, :], w2s[j][:CH[j], oc * 128:(oc + 1) * 128], hsb[j][:CH[j], :],
                        start=(j == 0), stop=(j == 21)))
                x.w("vector", "mm", x.c["mm"])
                x.op("vector", "vec", lambda e, oc=oc: e.tensor_scalar(
                    out=zz[:, :], in0=ps_z[:, :], scalar1=prm[:, 2 * DC + 44 + oc:2 * DC + 44 + oc + 1],
                    scalar2=None, op0=ALU.add))
                x.w("sync", "vec", x.c["vec"])
                x.dma("sync", "ld2", xr[:, :], XR_d[oc * 128:(oc + 1) * 128, t0:t0 + TS])
                x.w("vector", "ld2", x.c["ld2"])
                x.op("vector", "vec" if fin else None, lambda e: e.tensor_add(out=zz[:, :], in0=zz[:, :], in1=xr[:, :]))
                if fin:
                    x.w("sync", "vec", x.c["vec"])
                    x.dma("sync", "ld2", xr[:, :], X1_d[oc * 128:(oc + 1) * 128, t0:t0 + TS])
                    x.w("vector", "ld2", x.c["ld2"])
                    x.op("vector", "vec", lambda e: e.tensor_add(out=zz[:, :], in0=zz[:, :], in1=xr[:, :]))
                    x.w("sync", "vec", x.c["vec"])
                    x.dma("sync", "ld2", xr[:, :], A1_d[oc * 128:(oc + 1) * 128, t0:t0 + TS])
                    x.w("vector", "ld2", x.c["ld2"])
                    x.op("vector", "vec", lambda e: e.tensor_add(out=zz[:, :], in0=zz[:, :], in1=xr[:, :]))
                    x.w("gpsimd", "vec", x.c["vec"])
                    x.dma("gpsimd", "st", Z_d[oc * 128:(oc + 1) * 128, t0:t0 + TS], zz[:, :])
                else:
                    x.op("vector", None, lambda e: e.tensor_copy(out=zh[:, :], in_=zz[:, :]))
                    x.op("vector", "vec", lambda e: e.tensor_sub(out=zl[:, :], in0=zz[:, :], in1=zh[:, :]))
                    x.w("gpsimd", "vec", x.c["vec"])
                    x.dma("gpsimd", "st", Z_d[oc * 128:(oc + 1) * 128, t0:t0 + TS], zz[:, :])
                    x.dma("gpsimd", "st", Zh_d[oc * 128:(oc + 1) * 128, t0:t0 + TS], zh[:, :])
                    x.dma("gpsimd", "st", Zl_d[oc * 128:(oc + 1) * 128, t0:t0 + TS], zl[:, :])
                x.w("vector", "st", x.c["st"])
        x.w("gpsimd", "st")
        x.w("sync", "st")
    return make_block(nc, body)


def _t(a):
    return np.ascontiguousarray(a)


def kernel(L0_attn_x, L0_attn_t, L0_ffn_gate_w, L0_ffn_fc_w, L0_ffn_proj_w,
           L1_attn_Wq, L1_attn_Wk, L1_attn_Wv, L1_attn_Wg, L1_attn_bg,
           L1_attn_Wo, L1_attn_bo, L1_attn_ln_gamma, L1_attn_ln_beta,
           L1_ffn_wqkv, L1_ffn_q_scale, L1_ffn_k_scale, L1_ffn_wo,
           L1_ffn_ln_g, L1_ffn_ln_b, L1_ffn_w1, L1_ffn_b1, L1_ffn_w2, L1_ffn_b2,
           _dbg=None):
    cores = list(range(NC))
    ones128 = np.ones((128, 1), np.float32)
    onesT = np.ones((1, 128), np.float32)

    pw, _ = build_pw()
    pw_in = {
        "fcw": _t(L0_ffn_fc_w.reshape(8 * 1024, 4096)).astype(np.float32),
        "pjw": _t(L0_ffn_proj_w.reshape(8 * 4096, 1024)).astype(np.float32),
        "w1": _t(L1_ffn_w1.reshape(2 * 1024, 5460)).astype(np.float32),
        "w2": _t(L1_ffn_w2.reshape(2 * 2730, 1024)).astype(np.float32),
        "wqkv": _t(L1_ffn_wqkv.reshape(2 * 1024, 1536)).astype(np.float32),
        "wo": _t(L1_ffn_wo.reshape(2 * 512, 1024)).astype(np.float32),
        "wq": _t(L1_attn_Wq).astype(np.float32), "wk": _t(L1_attn_Wk).astype(np.float32),
        "wv": _t(L1_attn_Wv).astype(np.float32), "wg": _t(L1_attn_Wg).astype(np.float32),
        "woo": _t(L1_attn_Wo).astype(np.float32),
        "gw": _t(L0_ffn_gate_w).astype(np.float32),
    }
    rw = run_bass_kernel_spmd(pw, [pw_in], core_ids=[0]).results[0]

    x = np.asarray(L0_attn_x, np.float32)
    xT = _t(x.transpose(0, 2, 1))
    pa = build_pa()
    ra = run_bass_kernel_spmd(
        pa, [{"xT": xT[c * BPC:(c + 1) * BPC], "cb_ones": ones128, "cb_onesT": onesT} for c in cores],
        core_ids=cores).results
    xn_hi = np.concatenate([r["xn_hi"] for r in ra], 0)
    xn_lo = np.concatenate([r["xn_lo"] for r in ra], 0)
    if _dbg is not None:
        _dbg["xn_hi"], _dbg["xn_lo"] = xn_hi, xn_lo

    t = np.asarray(L0_attn_t, np.float32)
    tcm = np.zeros((1024, 4352), np.float32)
    tcm[:, 128:128 + 4096] = t.T
    def conv_x(arr):
        a = arr.reshape(16, 1024, 32, 128)[:, :, :, ::-1]
        return _t(a.transpose(1, 3, 2, 0))
    xch_all, xcl_all = conv_x(xn_hi), conv_x(xn_lo)
    pb = build_pb()
    yc = np.empty((1024, T, 16), np.float32)
    for half in range(2):
        sl = [slice((c * 2 + half) * CHB, (c * 2 + half + 1) * CHB) for c in cores]
        rb = run_bass_kernel_spmd(
            pb, [{"xch": xch_all[s], "xcl": xcl_all[s], "tcm": tcm[s]} for s in sl],
            core_ids=cores).results
        for c, s in enumerate(sl):
            yc[s] = rb[c]["yc"]
    yconvT = _t(yc.transpose(2, 0, 1))
    if _dbg is not None:
        _dbg["yconvT"] = yconvT

    onehot8 = np.zeros((8, 1024), np.float32)
    for e in range(8):
        onehot8[e, e * 128:(e + 1) * 128] = 1.0
    ident = np.eye(128, dtype=np.float32)
    identb = np.eye(128).astype(BFNP)
    hm2 = np.zeros((128, 2), np.float32)
    hm2[:64, 0] = 1.0
    hm2[64:, 1] = 1.0
    hm2T = _t(hm2.T)

    pc = build_pc()
    cstc = {"cb_ones": ones128, "cb_onesT": onesT, "cb_onehot": onehot8, "cb_ident": ident}
    moe_w = {"fcw": rw["fcw_bf"].reshape(8, 1024, 4096), "pjw": rw["pjw_bf"].reshape(8, 4096, 1024),
             "gwh": rw["gw_hi"], "gwl": rw["gw_lo"]}
    x1T = np.empty((16, 1024, 4096), np.float32)
    xn1_hi = np.empty((16, 1024, 4096), BFNP)
    xn1_lo = np.empty((16, 1024, 4096), BFNP)
    for k in range(2 * (4096 // NTOK)):
        bl, s0 = k // (4096 // NTOK), (k % (4096 // NTOK)) * NTOK
        maps = []
        for c in cores:
            bg_ = 2 * c + bl
            maps.append({"xnh": _t(xn_hi[bg_, :, s0:s0 + NTOK]), "xnl": _t(xn_lo[bg_, :, s0:s0 + NTOK]),
                         "xTs": _t(xT[bg_, :, s0:s0 + NTOK]), "ycs": _t(yconvT[bg_, :, s0:s0 + NTOK]),
                         **moe_w, **cstc})
        r = run_bass_kernel_spmd(pc, maps, core_ids=cores).results
        for c in cores:
            bg_ = 2 * c + bl
            x1T[bg_, :, s0:s0 + NTOK] = r[c]["x1s"]
            xn1_hi[bg_, :, s0:s0 + NTOK] = r[c]["xn1h"]
            xn1_lo[bg_, :, s0:s0 + NTOK] = r[c]["xn1l"]
    if _dbg is not None:
        _dbg["x1T"], _dbg["xn1_hi"], _dbg["xn1_lo"] = x1T, xn1_hi, xn1_lo

    g16 = 1.0 - 2.0 ** (-5.0 - np.arange(16, dtype=np.float64))
    qfold = np.empty((8, 128, 512), np.float32)
    kfold = np.empty((8, 128, 512), np.float32)
    cc = np.arange(512, dtype=np.float64)
    for p in range(8):
        for half64 in range(2):
            h = 2 * p + half64
            qfold[p, half64 * 64:(half64 + 1) * 64, :] = (g16[h] ** cc)[None, :]
            kfold[p, half64 * 64:(half64 + 1) * 64, :] = (g16[h] ** (-(cc % 128)))[None, :]
    cgt_half = []
    for hh0 in (0, 8):
        cg = np.zeros((128, 8 * 144), np.float32)
        for hloc in range(8):
            h = hh0 + hloc
            kk = 0
            for G in range(8):
                for TJ in range(min(4 * G + 3, 31) + 1):
                    with np.errstate(under="ignore"):
                        cg[:, hloc * 144 + kk] = np.float32(float(g16[h]) ** (128 * (4 * G - TJ)) / 8.0)
                    kk += 1
        cgt_half.append(cg)
    msk4 = np.zeros((4, 128, 512), np.float32)
    pp = np.arange(128)[:, None]
    ccc = np.arange(512)[None, :]
    for r in range(4):
        msk4[r] = (ccc >= 128 * r + pp).astype(np.float32)
    msk4 = msk4.astype(BFNP)

    pdproj = build_pdproj()
    pdc = [build_pdcore_block(0), build_pdcore_block(8)]
    pdln = build_pdln()
    col = lambda v: _t(np.asarray(v, np.float32).reshape(-1, 1))
    a1T = np.empty((16, 1024, 4096), np.float32)
    for bl in range(2):
        maps = [{"xnh": _t(xn1_hi[2 * c + bl]), "wq": rw["wq_bf"], "wk": rw["wk_bf"], "wv": rw["wv_bf"],
                 "wg": rw["wg_bf"], "bg": col(L1_attn_bg), "qfold": qfold, "kfold": kfold,
                 "cb_identb": identb} for c in cores]
        rp = run_bass_kernel_spmd(pdproj, maps, core_ids=cores).results
        rets = []
        for half in range(2):
            maps2 = [{"QT": rp[c]["QT"], "KT": rp[c]["KT"], "VT": rp[c]["VT"],
                      "cgt": cgt_half[half], "msk": msk4} for c in cores]
            rets.append(run_bass_kernel_spmd(pdc[half], maps2, core_ids=cores).results)
        maps3 = [{"RET": np.concatenate([rets[0][c]["RET"], rets[1][c]["RET"]], 0), "GT": rp[c]["GT"],
                  "woo": rw["woo_bf"], "bo": col(L1_attn_bo), "lng": col(L1_attn_ln_gamma),
                  "lnb": col(L1_attn_ln_beta), "cb_ones": ones128, "cb_onesT": onesT} for c in cores]
        rl = run_bass_kernel_spmd(pdln, maps3, core_ids=cores).results
        for c in cores:
            a1T[2 * c + bl] = rl[c]["A1"]
        if _dbg is not None and bl == 0:
            _dbg["ret_b0"] = np.concatenate([rets[0][0]["RET"], rets[1][0]["RET"]], 0)
    if _dbg is not None:
        _dbg["a1T"] = a1T

    def rot_tables():
        half = np.arange(0, 64, 2, dtype=np.float64)
        inv_freq = 1.0 / 10000.0 ** (half / 64.0)
        tt = np.arange(T, dtype=np.float64)
        fr = tt[:, None] * inv_freq[None, :]
        base = (half + 25.6) / 89.6
        power = (tt - 2048.0) / 256.0
        sc = base[None, :] ** power[:, None]
        cos, sin = np.cos(fr), np.sin(fr)
        def tile128(mat, sign_lo):
            r64 = np.concatenate([sign_lo * mat.T, mat.T], 0)
            return np.concatenate([r64, r64], 0).astype(np.float32)
        csq = tile128(cos * sc, +1.0)
        ssq = tile128(sin * sc, -1.0)
        csk = tile128(cos / sc, +1.0)
        ssk = tile128(sin / sc, -1.0)
        return _t(csq), _t(ssq), _t(csk), _t(ssk)
    csq, ssq, csk, ssk = rot_tables()
    amask = np.where((ccc[:, :640] if False else (np.arange(640)[None, :] >= pp) & (np.arange(640)[None, :] <= pp + 512)),
                     0.0, -3.0e38).astype(np.float32)
    peqkv = build_peqkv()
    peattn = build_peattn()
    pewo = build_pewo()
    pe2 = [build_pe2(False), build_pe2(True)]

    def shiftT(a):
        o = a.copy()
        o[..., 512:, 1:] = a[..., 512:, :-1]
        o[..., 512:, 0] = 0
        return o

    wqkv_bf = rw["wqkv_bf"].reshape(2, 1024, 1536)
    wo_bf = rw["wo_bf"].reshape(2, 512, 1024)
    w1_bf = rw["w1_bf"].reshape(2, 1024, 5460)
    w2_bf = rw["w2_bf"].reshape(2, 2730, 1024)
    qsc = np.asarray(L1_ffn_q_scale, np.float32)
    ksc = np.asarray(L1_ffn_k_scale, np.float32)
    b1 = np.asarray(L1_ffn_b1, np.float32)
    cur_hi, cur_lo = xn1_hi, xn1_lo
    outT = np.empty((16, 1024, 4096), np.float32)
    for l in range(2):
        h_hi = shiftT(cur_hi)
        qs_l = _t(np.tile(qsc[l], 2).reshape(128, 1))
        ks_l = _t(np.tile(ksc[l], 2).reshape(128, 1))
        b1a = np.zeros((128, 22), np.float32)
        b1g = np.zeros((128, 22), np.float32)
        for j in range(22):
            cj = min(128, 2730 - j * 128)
            b1a[:cj, j] = b1[l, j * 128:j * 128 + cj]
            b1g[:cj, j] = b1[l, 2730 + j * 128:2730 + j * 128 + cj]
        XA = np.empty((16, 1024, 4096), np.float32)
        for bl in range(2):
            maps = [{"h0": _t(h_hi[2 * c + bl]), "wqkv": wqkv_bf[l], "qs": qs_l, "ks": ks_l,
                     "csq": csq, "ssq": ssq, "csk": csk, "ssk": ssk,
                     "cb_hm2": hm2, "cb_hm2T": hm2T, "cb_identb": identb} for c in cores]
            rq = run_bass_kernel_spmd(peqkv, maps, core_ids=cores).results
            maps = [{"QR": rq[c]["QR"], "KR": rq[c]["KR"], "VR": rq[c]["VR"],
                     "amask": amask, "cb_identb": identb} for c in cores]
            ra2 = run_bass_kernel_spmd(peattn, maps, core_ids=cores).results
            maps = [{"OT": ra2[c]["OT"], "wo": wo_bf[l], "resh": _t(cur_hi[2 * c + bl]),
                     "resl": _t(cur_lo[2 * c + bl])} for c in cores]
            rw2 = run_bass_kernel_spmd(pewo, maps, core_ids=cores).results
            for c in cores:
                XA[2 * c + bl] = rw2[c]["XA"]
            if _dbg is not None and bl == 0 and l == 0:
                _dbg["QR0"], _dbg["KR0"], _dbg["VR0"], _dbg["OT0"] = rq[0]["QR"], rq[0]["KR"], rq[0]["VR"], ra2[0]["OT"]
        if _dbg is not None:
            _dbg[f"XA{l}"] = XA.copy()
        H1 = shiftT(XA)
        fin = (l == 1)
        nz_hi = np.empty((16, 1024, 4096), BFNP) if not fin else None
        nz_lo = np.empty((16, 1024, 4096), BFNP) if not fin else None
        for bl in range(2):
            maps = []
            for c in cores:
                bg_ = 2 * c + bl
                m = {"XR": _t(XA[bg_]), "H1": _t(H1[bg_]), "w1": w1_bf[l], "w2": w2_bf[l],
                     "b1a": b1a, "b1g": b1g, "b2": col(L1_ffn_b2[l]), "lg": col(L1_ffn_ln_g[l]),
                     "lb": col(L1_ffn_ln_b[l]), "cb_ones": ones128, "cb_onesT": onesT}
                if fin:
                    m["X1S"] = _t(x1T[bg_])
                    m["A1S"] = _t(a1T[bg_])
                maps.append(m)
            rz = run_bass_kernel_spmd(pe2[1 if fin else 0], maps, core_ids=cores).results
            for c in cores:
                bg_ = 2 * c + bl
                if fin:
                    outT[bg_] = rz[c]["Z"]
                else:
                    nz_hi[bg_] = rz[c]["Zh"]
                    nz_lo[bg_] = rz[c]["Zl"]
                    if _dbg is not None:
                        _dbg.setdefault("z1T", np.empty((16, 1024, 4096), np.float32))[bg_] = rz[c]["Z"]
        if not fin:
            cur_hi, cur_lo = nz_hi, nz_lo
    return _t(outT.transpose(0, 2, 1))
